# version 09
# baseline (speedup 1.0000x reference)
import numpy as np
import concourse.bass as bass
import concourse.mybir as mybir
from concourse.bass_utils import run_bass_kernel_spmd
from contextlib import ExitStack

F32 = mybir.dt.float32
BF16 = mybir.dt.bfloat16
F8 = mybir.dt.float8e4
F16 = mybir.dt.float16
AF = mybir.ActivationFunctionType
ALU = mybir.AluOpType

T = 2048
D = 1024
NH = 16
DH = 64
HID = 4096
NTT = 16
DC = 8
HC = 32
TB = 4
TKC = 16
EPS = 1e-5
CSHIFT = 12.0

ENGS = ("sync", "tensor", "vector", "scalar", "gpsimd")


def build_nc(dbg=()):
    nc = bass.Bass("TRN2", target_bir_lowering=False, debug=False,
                   detect_race_conditions=False)

    x_h = nc.dram_tensor("x", [T, D], F32, kind="ExternalInput").ap()
    wqk_h = nc.dram_tensor("wqk", [NH * D, 2 * DH], F32, kind="ExternalInput").ap()
    wvp_h = nc.dram_tensor("wvp", [NH // 2 * D, 2 * DH], F32, kind="ExternalInput").ap()
    wo_h = nc.dram_tensor("wo", [D, D], F32, kind="ExternalInput").ap()
    w1t_h = nc.dram_tensor("w1t", [D, HID], F32, kind="ExternalInput").ap()
    b1r_h = nc.dram_tensor("b1r", [128, HID // 128], F32, kind="ExternalInput").ap()
    w2t_h = nc.dram_tensor("w2t", [HID, D], F32, kind="ExternalInput").ap()
    b2r_h = nc.dram_tensor("b2r", [128, D // 128], F32, kind="ExternalInput").ap()
    eye_h = nc.dram_tensor("eye", [128, 192], F32, kind="ExternalInput").ap()
    out_h = nc.dram_tensor("out", [T, D], F32, kind="ExternalOutput").ap()
    dbg_h = {}
    for name, tap, shape, dt_ in dbg:
        dbg_h[name] = nc.dram_tensor("dbg_" + name, shape, dt_,
                                     kind="ExternalOutput").ap()

    with ExitStack() as stk:
        BASE = (nc.sbuf_base + 31) // 32 * 32
        alloc_log = []

        def at(name, shape, dtype, offset):
            sz = int(np.prod(shape[1:])) * mybir.dt.size(dtype)
            assert BASE + offset + sz <= nc.sbuf_top, (name, offset, sz)
            alloc_log.append((name, offset, sz))
            return nc.alloc_sbuf_tensor_at(name, shape, dtype, offset=BASE + offset)

        eye_s = at("eye_s", [128, 192], F32, 0)
        cells = at("cells", [128, 128], F32, 768)
        b1_s = at("b1_s", [128, HID // 128], F32, 1280)
        pv_s = at("pv_s", [128, DH], F32, 1408)
        lnt = at("lnt", [128, D], F32, 1696)
        xt = [at("xt%d" % i, [128, D], F32, 5792 + i * 4096) for i in range(2)]
        vT_s = at("vT_s", [128, T], F32, 1696)
        C_A1, C_A2 = slice(12, 14), slice(14, 16)
        C_M, C_V = slice(16, 17), slice(17, 18)
        C_T1, C_SQ = slice(18, 19), slice(19, 20)
        C_R = slice(20, 21)
        C_REC = slice(21, 22)
        C_EPS, C_NSH = slice(22, 23), slice(23, 24)
        C_B2Q, C_B2 = slice(24, 32), slice(32, 40)

        R = 14016
        qA = at("qA", [128, T], BF16, R)
        qB = at("qB", [128, T], BF16, R + 4096)
        kp = at("kp", [128, T], BF16, R + 8192)
        wqk_s = at("wqk_s", [128, D], F32, R + 12288)
        wqk_hi = at("wqk_hi", [128, D], BF16, R + 16384)
        wqk_lo = at("wqk_lo", [128, D], BF16, R + 18432)
        wv_s = at("wv_s", [128, D], F32, R + 20480)
        wv_hi = at("wv_hi", [128, D], BF16, R + 24576)
        wv_lo = at("wv_lo", [128, D], BF16, R + 26624)
        v_tok = at("v_tok", [128, TKC * (DH + 1)], F32, R + 28672)
        stg = at("stg", [64, 512], BF16, R + 32832)
        vstg = at("vstg", [64, 128], F32, R + 33856)
        wo_s = at("wo_s", [128, D], F32, R)
        wo_hi = at("wo_hi", [128, D], BF16, R + 4096)
        wo_lo = at("wo_lo", [128, D], BF16, R + 6144)
        scr512 = at("scr512", [128, 512], F32, R + 8192)
        w1_s = at("w1_s", [128, D], F32, R)
        w1_hi = at("w1_hi", [128, D], BF16, R + 4096)
        w1_lo = at("w1_lo", [128, D], BF16, R + 6144)
        oq1024 = at("oq1024", [128, D], F32, R + 4096)
        w2_s = at("w2_s", [128, 512], F32, R + 8192)
        h_s = at("h_s", [128, 512], F32, R + 10240)
        oq_s = at("oq_s", [128, 128], F32, R + 12288)
        QO = R + 12800
        q_u = at("q_u", [128, 512], BF16, QO)
        q_s = at("q_s", [128, 512], BF16, QO + 1024)
        q_b = at("q_b", [128, 512], BF16, QO + 2048)
        q_t = at("q_t", [128, 512], BF16, QO + 3072)
        q_tS = at("q_tS", [128, 512], BF16, QO + 4096)
        q_inv = at("q_inv", [128, 512], BF16, QO + 5120)
        q_scl = at("q_scl", [128, 512], BF16, QO + 6144)
        q_sg = at("q_sg", [128, 512], BF16, QO + 7168)
        q_ax = at("q_ax", [128, 512], F32, QO + 8192)
        q_y0 = at("q_y0", [128, 512], F32, QO + 10240)
        q_yr = at("q_yr", [128, 512], F32, QO + 12288)
        q_rr = at("q_rr", [128, 512], F32, QO + 14336)
        q_f16 = at("q_f16", [128, 512], F16, QO + 16384)

        S1 = R + 34400
        ln1T_hi = at("ln1T_hi", [128, DC * T], BF16, S1)
        ln1T_lo = at("ln1T_lo", [128, DC * T], BF16, S1 + 32768)
        x2 = at("x2", [128, NTT * D], F32, S1)
        S2 = S1 + 65536
        attnT_hi = at("attnT_hi", [128, DC * T], BF16, S2)
        attnT_lo = at("attnT_lo", [128, DC * T], BF16, S2 + 32768)
        lnfT_hi = at("lnfT_hi", [128, DC * 1024], BF16, S2)
        lnfT_lo = at("lnfT_lo", [128, DC * 1024], BF16, S2 + 16384)
        hq_half = at("hq_half", [128, HC * 1024], F8, S2 + 32768)
        S3 = S2 + 65536
        e_s = at("e_s", [128, TKC * 512], F32, S3)
        w2q = at("w2q", [128, HC * D], F8, S3)

        P512 = [stk.enter_context(nc.psum_tensor("p512_%d" % i, [128, 512], F32))
                for i in range(4)]
        P1024 = [stk.enter_context(nc.psum_tensor("p1024_%d" % i, [128, 1024], F32))
                 for i in range(2)]

        sems = {k: stk.enter_context(nc.semaphore("sem_" + k))
                for k in ("LD", "GD", "TEN", "VEC", "SCA")}
        block = stk.enter_context(nc.Block())

        def program(me, E):
            cnt = {k: 0 for k in sems}
            guards = {}

            def emit(owner, fn, sem=None, k=1):
                ins = fn(E) if me == owner else None
                if sem is not None:
                    if ins is not None:
                        ins.then_inc(sems[sem], k)
                    cnt[sem] += k
                return ins

            def wait(owner, sem, val=None):
                if me == owner:
                    E.wait_ge(sems[sem], cnt[sem] if val is None else val)

            def set_guard(key, sem):
                guards[key] = (sem, cnt[sem])

            def use_guard(key, owner):
                if key in guards:
                    s, v = guards[key]
                    wait(owner, s, v)

            def dbg_tap(label, tensors):
                names = [n for n, tap, _s, _d in dbg if tap == label]
                if not names:
                    return
                wait("gpsimd", "VEC")
                wait("gpsimd", "SCA")
                wait("gpsimd", "TEN")
                for name, tap, shape, dt_ in dbg:
                    if tap != label:
                        continue
                    src = tensors[name]
                    emit("gpsimd", lambda e, name=name, src=src: e.dma_start(
                        out=dbg_h[name], in_=src), "GD", 16)
                for o in ENGS:
                    wait(o, "GD")

            def vpad(n=1):
                for _ in range(n):
                    emit("vector", lambda e: e.tensor_copy(
                        cells[:, 48:80], eye_s[:, 0:32]), "VEC")

            def split_fp32(src_ap, hi_ap, lo_ap):
                emit("vector", lambda e: e.tensor_copy(hi_ap, src_ap), "VEC")
                emit("vector", lambda e: e.scalar_tensor_tensor(
                    lo_ap, src_ap, 1.0, hi_ap, ALU.mult, ALU.subtract), "VEC")

            def quant_cascade(F, src_ap, kind, out_ap):
                man = 3 if kind.startswith("e4m3") else 10
                ebase = -6 if kind.startswith("e4m3") else -14
                ebits = [8, 4, 2, 1] if man == 3 else [16, 8, 4, 2, 1]
                rbits = [8, 4, 2, 1] if man == 3 else \
                    [1024, 512, 256, 128, 64, 32, 16, 8, 4, 2, 1]
                sl = (slice(0, 128), slice(0, F))
                signed = not kind.endswith("pos")
                if F < 32:
                    def v(f):
                        emit("vector", f, "VEC")
                        vpad()
                else:
                    v = lambda f: emit("vector", f, "VEC")
                if signed:
                    v(lambda e: e.tensor_scalar(q_s[sl], src_ap, -(2.0 ** 60),
                                                0.0, ALU.mult, ALU.add))
                    v(lambda e: e.tensor_scalar(q_b[sl], q_s[sl], 0.0, 1.0,
                                                ALU.max, ALU.min))
                    v(lambda e: e.tensor_scalar(q_sg[sl], q_b[sl], -2.0, 1.0,
                                                ALU.mult, ALU.add))
                    v(lambda e: e.tensor_tensor(q_ax[sl], src_ap, q_sg[sl],
                                                ALU.mult))
                    ax = q_ax[sl]
                else:
                    ax = src_ap
                v(lambda e: e.tensor_copy(q_u[sl], ax))
                first = True
                for c in ebits:
                    Tc = 2.0 ** (ebase + c)
                    Mc = 2.0 ** 40 / Tc
                    v(lambda e, Mc=Mc: e.tensor_scalar(
                        q_s[sl], q_u[sl], Mc, -(2.0 ** 40), ALU.mult, ALU.add))
                    v(lambda e: e.tensor_scalar(q_b[sl], q_s[sl], 0.0, 1.0,
                                                ALU.max, ALU.min))
                    v(lambda e, c=c: e.tensor_scalar(
                        q_t[sl], q_b[sl], float(2.0 ** (-c) - 1.0), 1.0,
                        ALU.mult, ALU.add))
                    v(lambda e, c=c: e.tensor_scalar(
                        q_tS[sl], q_b[sl], float(2.0 ** c - 1.0), 1.0,
                        ALU.mult, ALU.add))
                    v(lambda e: e.tensor_mul(q_u[sl], q_u[sl], q_t[sl]))
                    if first:
                        iv0 = float(2.0 ** (man - ebase))
                        v(lambda e, iv0=iv0: e.tensor_scalar(
                            q_inv[sl], q_t[sl], iv0, 0.0, ALU.mult, ALU.add))
                        v(lambda e, iv0=iv0: e.tensor_scalar(
                            q_scl[sl], q_tS[sl], 1.0 / iv0, 0.0,
                            ALU.mult, ALU.add))
                        first = False
                    else:
                        v(lambda e: e.tensor_mul(q_inv[sl], q_inv[sl], q_t[sl]))
                        v(lambda e: e.tensor_mul(q_scl[sl], q_scl[sl], q_tS[sl]))
                v(lambda e: e.tensor_mul(q_y0[sl], ax, q_inv[sl]))
                srcy = q_y0
                for c in rbits:
                    Mc = 2.0 ** 40 / c
                    v(lambda e, Mc=Mc, srcy=srcy: e.tensor_scalar(
                        q_s[sl], srcy[sl], Mc, -(2.0 ** 40), ALU.mult, ALU.add))
                    v(lambda e: e.tensor_scalar(q_b[sl], q_s[sl], 0.0, 1.0,
                                                ALU.max, ALU.min))
                    if c == 1:
                        v(lambda e: e.tensor_copy(q_tS[sl], q_b[sl]))
                    v(lambda e, c=c, srcy=srcy: e.scalar_tensor_tensor(
                        q_yr[sl], q_b[sl], -float(c), srcy[sl],
                        ALU.mult, ALU.add))
                    srcy = q_yr
                v(lambda e: e.tensor_scalar(q_s[sl], q_yr[sl], 2.0 ** 41,
                                            -(2.0 ** 40), ALU.mult, ALU.add))
                v(lambda e: e.tensor_scalar(q_b[sl], q_s[sl], 0.0, 1.0,
                                            ALU.max, ALU.min))
                v(lambda e: e.tensor_scalar(q_s[sl], q_yr[sl], -(2.0 ** 41),
                                            2.0 ** 40, ALU.mult, ALU.add))
                v(lambda e: e.tensor_scalar(q_t[sl], q_s[sl], 0.0, 1.0,
                                            ALU.max, ALU.min))
                v(lambda e: e.tensor_scalar(q_t[sl], q_t[sl], -1.0, 1.0,
                                            ALU.mult, ALU.add))
                v(lambda e: e.tensor_sub(q_t[sl], q_t[sl], q_b[sl]))
                v(lambda e: e.tensor_mul(q_t[sl], q_t[sl], q_tS[sl]))
                v(lambda e: e.tensor_add(q_b[sl], q_b[sl], q_t[sl]))
                v(lambda e: e.tensor_sub(q_rr[sl], q_y0[sl], q_yr[sl]))
                v(lambda e: e.scalar_tensor_tensor(q_rr[sl], q_b[sl], 1.0,
                                                   q_rr[sl], ALU.mult, ALU.add))
                v(lambda e: e.tensor_mul(q_rr[sl], q_rr[sl], q_scl[sl]))
                if signed:
                    v(lambda e: e.tensor_mul(q_rr[sl], q_rr[sl], q_sg[sl]))
                if man == 3:
                    v(lambda e: e.tensor_copy(out_ap, q_rr[sl]))
                else:
                    v(lambda e: e.tensor_copy(q_f16[sl], q_rr[sl]))
                    v(lambda e: e.tensor_copy(out_ap, q_f16[sl]))

            def ln_stats(src_ap1024):
                wait("vector", "SCA")
                v = lambda f: emit("vector", f, "VEC")
                v(lambda e: e.bn_stats(cells[:, 0:6], src_ap1024[:, 0:512]))
                v(lambda e: e.bn_stats(cells[:, 6:12], src_ap1024[:, 512:1024]))
                v(lambda e: e.bn_aggr(cells[:, 12:14], cells[:, 0:6]))
                v(lambda e: e.bn_aggr(cells[:, 14:16], cells[:, 6:12]))
                v(lambda e: e.tensor_mul(cells[:, 18:19], cells[:, 12:13],
                                         cells[:, 12:13]))
                v(lambda e: e.tensor_mul(cells[:, 40:41], cells[:, 14:15],
                                         cells[:, 14:15]))
                v(lambda e: e.tensor_add(cells[:, 41:42], cells[:, 12:13],
                                         cells[:, 14:15]))
                v(lambda e: e.tensor_add(cells[:, 18:19], cells[:, 18:19],
                                         cells[:, 13:14]))
                v(lambda e: e.tensor_add(cells[:, 40:41], cells[:, 40:41],
                                         cells[:, 15:16]))
                v(lambda e: e.tensor_scalar(cells[:, C_M], cells[:, 41:42],
                                            0.5, 0.0, ALU.mult, ALU.add))
                v(lambda e: e.tensor_add(cells[:, 18:19], cells[:, 18:19],
                                         cells[:, 40:41]))
                v(lambda e: e.tensor_mul(cells[:, 19:20], cells[:, C_M],
                                         cells[:, C_M]))
                vpad()
                v(lambda e: e.scalar_tensor_tensor(
                    cells[:, C_V], cells[:, 18:19], 0.5, cells[:, 19:20],
                    ALU.mult, ALU.subtract))
                v(lambda e: e.tensor_copy(cells[:, 48:80], eye_s[:, 0:32]))
                wait("scalar", "VEC")
                emit("scalar", lambda e: e.activation(
                    cells[:, 42:43], cells[:, C_V], AF.Sqrt,
                    bias=cells[:, C_EPS], scale=1.0), "SCA")
                wait("vector", "SCA")
                vpad()
                v(lambda e: e.reciprocal(cells[:, C_R], cells[:, 42:43]))
                vpad()

            def ln_apply(src_ap1024, dst_ap1024):
                emit("vector", lambda e: e.tensor_scalar(
                    dst_ap1024, src_ap1024, cells[:, C_M], cells[:, C_R],
                    ALU.subtract, ALU.mult), "VEC")

            def transpose_split8(srcs, his, los):
                use_guard("pt", "tensor")
                for k in range(8):
                    emit("tensor", lambda e, k=k: e.transpose(
                        P1024[1][0:128, k * 128:(k + 1) * 128], srcs[k],
                        eye_s[:, 0:128]), "TEN" if k == 7 else None)
                wait("vector", "TEN")
                for k in range(8):
                    emit("vector", lambda e, k=k: e.tensor_copy(
                        his[k], P1024[1][0:128, k * 128:(k + 1) * 128]), "VEC")
                for k in range(8):
                    emit("vector", lambda e, k=k: e.scalar_tensor_tensor(
                        los[k], P1024[1][0:128, k * 128:(k + 1) * 128], 1.0,
                        his[k], ALU.mult, ALU.subtract), "VEC")
                set_guard("pt", "VEC")

            emit("sync", lambda e: e.dma_start(out=eye_s[:], in_=eye_h), "LD", 16)
            emit("sync", lambda e: e.dma_start(out=b1_s[:], in_=b1r_h), "LD", 16)
            emit("sync", lambda e: e.dma_start(out=cells[:, C_B2], in_=b2r_h),
                 "LD", 16)
            emit("vector", lambda e: e.memset(v_tok[:], 1.0), "VEC")
            emit("vector", lambda e: e.memset(cells[:, C_EPS], EPS), "VEC")
            emit("vector", lambda e: e.memset(cells[:, C_NSH], -CSHIFT), "VEC")

            for i in range(NTT):
                use_guard("xt%d" % (i % 2), "sync")
                emit("sync", lambda e, i=i: e.dma_start(
                    out=xt[i % 2][:], in_=x_h[i * 128:(i + 1) * 128, :]), "LD", 16)
                wait("vector", "LD")
                ln_stats(xt[i % 2][:])
                use_guard("lnt_rd", "vector")
                ln_apply(xt[i % 2][:], lnt[:])
                set_guard("xt%d" % (i % 2), "VEC")
                wait("tensor", "VEC")
                transpose_split8(
                    [lnt[:, k * 128:(k + 1) * 128] for k in range(DC)],
                    [ln1T_hi[:, k * T + i * 128:k * T + (i + 1) * 128]
                     for k in range(DC)],
                    [ln1T_lo[:, k * T + i * 128:k * T + (i + 1) * 128]
                     for k in range(DC)])
                set_guard("lnt_rd", "TEN")

            dbg_tap("p0", {"ln1T_hi": ln1T_hi[:], "ln1T_lo": ln1T_lo[:],
                           "lnt": lnt[:], "stb": cells[:, 0:24],
                           "st_r": cells[:, C_R], "xt1": xt[1][:]})
            for h in range(NH):
                par = h % 2
                use_guard("wqk_s", "sync")
                for k in range(DC):
                    emit("sync", lambda e, h=h, k=k: e.dma_start(
                        out=wqk_s[:, k * 128:(k + 1) * 128],
                        in_=wqk_h[h * D + k * 128:h * D + (k + 1) * 128, :]),
                        "LD", 16)
                use_guard("wqk_hi", "vector")
                wait("vector", "LD")
                split_fp32(wqk_s[:], wqk_hi[:], wqk_lo[:])
                set_guard("wqk_s", "VEC")
                if par == 0:
                    use_guard("wv_s", "sync")
                    for k in range(DC):
                        emit("sync", lambda e, h=h, k=k: e.dma_start(
                            out=wv_s[:, k * 128:(k + 1) * 128],
                            in_=wvp_h[(h // 2) * D + k * 128:
                                      (h // 2) * D + (k + 1) * 128, :]), "LD", 16)
                    use_guard("wv_hi", "vector")
                    wait("vector", "LD")
                    split_fp32(wv_s[:], wv_hi[:], wv_lo[:])
                    set_guard("wv_s", "VEC")
                wait("tensor", "VEC")
                for tb in range(TB):
                    tsl = slice(tb * 512, (tb + 1) * 512)
                    use_guard("pqk", "tensor")
                    nmm = 0
                    for wl, ll in ((wqk_hi, ln1T_hi), (wqk_hi, ln1T_lo),
                                   (wqk_lo, ln1T_hi)):
                        for k in range(DC):
                            nmm += 1
                            emit("tensor",
                                 lambda e, wl=wl, ll=ll, k=k, tb=tb, nmm=nmm:
                                 e.matmul(
                                     P512[0][:],
                                     wl[:, k * 128:(k + 1) * 128],
                                     ll[:, k * T + tb * 512:k * T + (tb + 1) * 512],
                                     start=(nmm == 1), stop=(nmm == 24)),
                                 "TEN" if nmm == 24 else None)
                    wait("vector", "TEN")
                    emit("vector", lambda e, tsl=tsl: e.tensor_copy(
                        qA[0:64, tsl], P512[0][0:64, :]), "VEC")
                    emit("vector", lambda e, tsl=tsl: e.tensor_copy(
                        qA[64:128, tsl], qA[0:64, tsl]), "VEC")
                    emit("vector", lambda e, tsl=tsl: e.scalar_tensor_tensor(
                        qB[0:64, tsl], P512[0][0:64, :], 1.0, qA[0:64, tsl],
                        ALU.mult, ALU.subtract), "VEC")
                    emit("vector", lambda e, tsl=tsl: e.tensor_copy(
                        qB[64:128, tsl], qB[0:64, tsl]), "VEC")
                    emit("vector", lambda e, tsl=tsl: e.tensor_copy(
                        kp[0:64, tsl], P512[0][64:128, :]), "VEC")
                    emit("vector", lambda e, tsl=tsl: e.scalar_tensor_tensor(
                        kp[64:128, tsl], P512[0][64:128, :], 1.0, kp[0:64, tsl],
                        ALU.mult, ALU.subtract), "VEC")
                    set_guard("pqk", "VEC")
                    if par == 0:
                        use_guard("pv", "tensor")
                        nmm = 0
                        for wl, ll in ((wv_hi, ln1T_hi), (wv_hi, ln1T_lo),
                                       (wv_lo, ln1T_hi)):
                            for k in range(DC):
                                nmm += 1
                                emit("tensor",
                                     lambda e, wl=wl, ll=ll, k=k, tb=tb, nmm=nmm:
                                     e.matmul(
                                         P1024[0][:, 0:512],
                                         wl[:, k * 128:(k + 1) * 128],
                                         ll[:, k * T + tb * 512:
                                            k * T + (tb + 1) * 512],
                                         start=(nmm == 1), stop=(nmm == 24)),
                                     "TEN" if nmm == 24 else None)
                        use_guard("vT_s", "vector")
                        wait("vector", "TEN")
                        emit("vector", lambda e, tsl=tsl: e.tensor_copy(
                            vT_s[:, tsl], P1024[0][:, 0:512]), "VEC")
                        set_guard("pv", "VEC")
                if par == 0:
                    set_guard("wv_hi", "TEN")
                set_guard("wqk_hi", "TEN")
                if h == 0:
                    dbg_tap("vearly", {"vT_sE": vT_s[:], "wv_hiE": wv_hi[:],
                                       "wv_loE": wv_lo[:], "wv_sE": wv_s[:]})
                use_guard("v_tok", "tensor")
                wait("tensor", "VEC")
                for c in range(TKC):
                    if par:
                        use_guard("vstg", "vector")
                        emit("vector", lambda e, c=c: e.tensor_copy(
                            vstg[0:64, 0:128],
                            vT_s[64:128, c * 128:(c + 1) * 128]), "VEC")
                        wait("tensor", "VEC")
                    use_guard("pt", "tensor")
                    emit("tensor", lambda e, c=c, par=par: e.transpose(
                        P1024[1][0:128, 0:64],
                        vstg[0:64, 0:128] if par else
                        vT_s[0:64, c * 128:(c + 1) * 128],
                        eye_s[0:64, 0:64]), "TEN")
                    if par:
                        set_guard("vstg", "TEN")
                    wait("vector", "TEN")
                    emit("vector", lambda e, c=c: e.tensor_copy(
                        v_tok[:, c * 65:c * 65 + 64], P1024[1][0:128, 0:64]),
                        "VEC")
                    set_guard("pt", "VEC")
                if par == 1:
                    set_guard("vT_s", "TEN")
                for tb in range(TB):
                    tsl = slice(tb * 512, (tb + 1) * 512)
                    wait("tensor", "VEC")

                    def av_mm(c):
                        wait("tensor", "SCA")
                        if c == 0:
                            use_guard("pv", "tensor")
                        emit("tensor", lambda e, c=c: e.matmul(
                            P1024[0][0:65, 0:512],
                            v_tok[:, c * 65:(c + 1) * 65],
                            e_s[:, c * 512:(c + 1) * 512],
                            start=(c == 0), stop=(c == TKC - 1),
                            skip_group_check=True),
                            "TEN" if c == TKC - 1 else None)

                    for c in range(TKC):
                        ps = P512[2 + (c % 2)]
                        use_guard("ps%d" % (c % 2), "tensor")
                        emit("tensor", lambda e, c=c, ps=ps, tsl=tsl: e.matmul(
                            ps[:], kp[:, c * 128:(c + 1) * 128], qA[:, tsl],
                            start=True, stop=False, skip_group_check=True))
                        emit("tensor", lambda e, c=c, ps=ps, tsl=tsl: e.matmul(
                            ps[:], kp[:, c * 128:(c + 1) * 128], qB[:, tsl],
                            start=False, stop=True, skip_group_check=True),
                            "TEN")
                        wait("scalar", "TEN")
                        emit("scalar", lambda e, c=c, ps=ps: e.activation(
                            e_s[:, c * 512:(c + 1) * 512], ps[:], AF.Exp,
                            bias=cells[:, C_NSH], scale=0.125), "SCA")
                        set_guard("ps%d" % (c % 2), "SCA")
                        if c >= 1:
                            av_mm(c - 1)
                    av_mm(TKC - 1)
                    use_guard("dnrow", "vector")
                    wait("vector", "TEN")
                    emit("vector", lambda e: e.tensor_copy(
                        xt[1][64:65, 0:512], P1024[0][64:65, 0:512]), "VEC")
                    vpad()
                    emit("vector", lambda e: e.reciprocal(
                        xt[1][0:1, 512:1024], xt[1][64:65, 0:512]), "VEC")
                    use_guard("ps0", "tensor")
                    wait("tensor", "VEC")
                    emit("tensor", lambda e: e.matmul(
                        P512[2][0:64, 0:512], eye_s[0:1, 128:192],
                        xt[1][0:1, 512:1024], start=True, stop=True), "TEN")
                    set_guard("dnrow", "TEN")
                    wait("vector", "TEN")
                    emit("vector", lambda e: e.tensor_copy(
                        xt[1][0:64, 0:512], P512[2][0:64, 0:512]), "VEC")
                    set_guard("ps0", "VEC")
                    coff0 = (h // 2) * T + tb * 512
                    use_guard("ps1", "vector")
                    if par == 0:
                        emit("vector", lambda e: e.tensor_tensor(
                            P512[3][0:64, 0:512], P1024[0][0:64, 0:512],
                            xt[1][0:64, 0:512], ALU.mult), "VEC")
                        emit("vector", lambda e, coff0=coff0: e.tensor_tensor(
                            attnT_hi[0:64, coff0:coff0 + 512],
                            P1024[0][0:64, 0:512], xt[1][0:64, 0:512],
                            ALU.mult), "VEC")
                        set_guard("pv", "VEC")
                        vpad()
                        emit("vector", lambda e, coff0=coff0:
                             e.scalar_tensor_tensor(
                                 attnT_lo[0:64, coff0:coff0 + 512],
                                 P512[3][0:64, 0:512], 1.0,
                                 attnT_hi[0:64, coff0:coff0 + 512],
                                 ALU.mult, ALU.subtract), "VEC")
                    else:
                        emit("vector", lambda e: e.tensor_tensor(
                            P512[3][0:64, 0:512], P1024[0][0:64, 0:512],
                            xt[1][0:64, 0:512], ALU.mult), "VEC")
                        emit("vector", lambda e: e.tensor_tensor(
                            stg[0:64, 0:512], P1024[0][0:64, 0:512],
                            xt[1][0:64, 0:512], ALU.mult), "VEC")
                        set_guard("pv", "VEC")
                        vpad()
                        emit("vector", lambda e, coff0=coff0: e.tensor_copy(
                            attnT_hi[64:128, coff0:coff0 + 512],
                            stg[0:64, 0:512]), "VEC")
                        emit("vector", lambda e: e.scalar_tensor_tensor(
                            stg[0:64, 0:512], P512[3][0:64, 0:512], 1.0,
                            stg[0:64, 0:512], ALU.mult, ALU.subtract), "VEC")
                        vpad()
                        emit("vector", lambda e, coff0=coff0: e.tensor_copy(
                            attnT_lo[64:128, coff0:coff0 + 512],
                            stg[0:64, 0:512]), "VEC")
                    set_guard("ps1", "VEC")

            for jc in range(DC):
                use_guard("wo_s", "sync")
                wait("sync", "TEN")
                for k in range(DC):
                    emit("sync", lambda e, jc=jc, k=k: e.dma_start(
                        out=wo_s[:, k * 128:(k + 1) * 128],
                        in_=wo_h[k * 128:(k + 1) * 128,
                                 jc * 128:(jc + 1) * 128]), "LD", 16)
                use_guard("wo_hi", "vector")
                wait("vector", "LD")
                split_fp32(wo_s[:], wo_hi[:], wo_lo[:])
                set_guard("wo_s", "VEC")
                wait("tensor", "VEC")
                for tb in range(TB):
                    use_guard("pwo", "tensor")
                    nmm = 0
                    for wl, al in ((wo_hi, attnT_hi), (wo_hi, attnT_lo),
                                   (wo_lo, attnT_hi)):
                        for k in range(DC):
                            nmm += 1
                            emit("tensor", lambda e, wl=wl, al=al, k=k, tb=tb,
                                 nmm=nmm: e.matmul(
                                     P512[0][:], wl[:, k * 128:(k + 1) * 128],
                                     al[:, k * T + tb * 512:k * T + (tb + 1) * 512],
                                     start=(nmm == 1), stop=(nmm == 24)),
                                 "TEN" if nmm == 24 else None)
                    use_guard("scr512", "vector")
                    wait("vector", "TEN")
                    emit("vector", lambda e: e.tensor_copy(scr512[:], P512[0][:]),
                         "VEC")
                    set_guard("pwo", "VEC")
                    if jc == 0 and tb == 0:
                        dbg_tap("wo0", {"scr512E": scr512[:]})
                    wait("tensor", "VEC")
                    for tt in range(4):
                        use_guard("pt", "tensor")
                        emit("tensor", lambda e, tt=tt: e.transpose(
                            P1024[1][0:128, 0:128],
                            scr512[:, tt * 128:(tt + 1) * 128],
                            eye_s[:, 0:128]), "TEN")
                        wait("vector", "TEN")
                        ti = tb * 4 + tt
                        emit("vector", lambda e, ti=ti, jc=jc: e.tensor_copy(
                            x2[:, ti * D + jc * 128:ti * D + (jc + 1) * 128],
                            P1024[1][0:128, 0:128]), "VEC")
                        set_guard("pt", "VEC")
                    set_guard("scr512", "TEN")
                set_guard("wo_hi", "TEN")
            dbg_tap("p2pre", {"x2pre": x2[:]})
            for i in range(NTT):
                use_guard("xt%d" % (i % 2), "gpsimd")
                wait("gpsimd", "TEN")
                emit("gpsimd", lambda e, i=i: e.dma_start(
                    out=xt[i % 2][:], in_=x_h[i * 128:(i + 1) * 128, :]), "GD", 16)
                dsl = slice(i * D, (i + 1) * D)
                wait("vector", "TEN")
                ln_stats(x2[:, dsl])
                use_guard("lnt_rd", "vector")
                ln_apply(x2[:, dsl], lnt[:])
                wait("vector", "GD")
                emit("vector", lambda e, dsl=dsl, i=i: e.tensor_add(
                    x2[:, dsl], lnt[:], xt[i % 2][:]), "VEC")
                set_guard("lnt_rd", "VEC")
                set_guard("xt%d" % (i % 2), "VEC")

            wait("sync", "TEN")
            wait("vector", "TEN")
            quant_cascade(D // 128, cells[:, C_B2], "e5m10", cells[:, C_B2Q])
            for half in range(2):
                hof = half * 1024
                for i in range(8):
                    ti = half * 8 + i
                    dsl = slice(ti * D, (ti + 1) * D)
                    ln_stats(x2[:, dsl])
                    use_guard("lnt_rd", "vector")
                    ln_apply(x2[:, dsl], lnt[:])
                    wait("tensor", "VEC")
                    transpose_split8(
                        [lnt[:, k * 128:(k + 1) * 128] for k in range(DC)],
                        [lnfT_hi[:, k * 1024 + i * 128:k * 1024 + (i + 1) * 128]
                         for k in range(DC)],
                        [lnfT_lo[:, k * 1024 + i * 128:k * 1024 + (i + 1) * 128]
                         for k in range(DC)])
                    set_guard("lnt_rd", "TEN")
                for hc in range(HC):
                    use_guard("w1_s", "sync")
                    for k in range(DC):
                        emit("sync", lambda e, hc=hc, k=k: e.dma_start(
                            out=w1_s[:, k * 128:(k + 1) * 128],
                            in_=w1t_h[k * 128:(k + 1) * 128,
                                      hc * 128:(hc + 1) * 128]), "LD", 16)
                    use_guard("w1_hi", "vector")
                    wait("vector", "LD")
                    split_fp32(w1_s[:], w1_hi[:], w1_lo[:])
                    set_guard("w1_s", "VEC")
                    wait("tensor", "VEC")
                    for tb in range(2):
                        pf = P512[tb % 2]
                        use_guard("ph%d" % (tb % 2), "tensor")
                        nmm = 0
                        for wl, ll in ((w1_hi, lnfT_hi), (w1_hi, lnfT_lo),
                                       (w1_lo, lnfT_hi)):
                            for k in range(DC):
                                nmm += 1
                                emit("tensor",
                                     lambda e, wl=wl, ll=ll, k=k, tb=tb, pf=pf,
                                     nmm=nmm: e.matmul(
                                         pf[:], wl[:, k * 128:(k + 1) * 128],
                                         ll[:, k * 1024 + tb * 512:
                                            k * 1024 + (tb + 1) * 512],
                                         start=(nmm == 1), stop=(nmm == 24)),
                                     "TEN" if nmm == 24 else None)
                        use_guard("h_s", "scalar")
                        wait("scalar", "TEN")
                        emit("scalar", lambda e, hc=hc, pf=pf: e.activation(
                            h_s[:], pf[:], AF.Relu, bias=b1_s[:, hc:hc + 1],
                            scale=1.0), "SCA")
                        set_guard("ph%d" % (tb % 2), "SCA")
                        wait("vector", "SCA")
                        quant_cascade(
                            512, h_s[:], "e4m3pos",
                            hq_half[:, hc * 1024 + tb * 512:
                                    hc * 1024 + (tb + 1) * 512])
                        set_guard("h_s", "VEC")
                    set_guard("w1_hi", "TEN")
                    if half == 0:
                        use_guard("w2_s", "sync")
                        for piece in range(2):
                            emit("sync", lambda e, hc=hc, piece=piece: e.dma_start(
                                out=w2_s[:],
                                in_=w2t_h[hc * 128:(hc + 1) * 128,
                                          piece * 512:(piece + 1) * 512]),
                                "LD", 16)
                            wait("vector", "LD")
                            quant_cascade(
                                512, w2_s[:], "e4m3",
                                w2q[:, hc * D + piece * 512:
                                    hc * D + (piece + 1) * 512])
                            set_guard("w2_s", "VEC")
                            use_guard("w2_s", "sync")
                if half == 0:
                    dbg_tap("p4a", {"hq_half": hq_full[:] if False else hq_half[:],
                                    "w2qE": w2q[:], "cellsE": cells[:]})
                for tb2 in range(4):
                    wait("tensor", "VEC")
                    use_guard("po", "tensor")
                    for hc in range(HC):
                        for jc in range(DC):
                            po = P1024[jc // 4][:, (jc % 4) * 256:
                                                (jc % 4) * 256 + 256]
                            emit("tensor", lambda e, hc=hc, jc=jc, tb2=tb2, po=po:
                                 e.matmul(
                                     po,
                                     w2q[:, hc * D + jc * 128:
                                         hc * D + (jc + 1) * 128],
                                     hq_half[:, hc * 1024 + tb2 * 256:
                                             hc * 1024 + (tb2 + 1) * 256],
                                     start=(hc == 0 and jc % 2 == 0),
                                     stop=(hc == HC - 1 and jc % 2 == 1),
                                     skip_group_check=True),
                                 "TEN" if (hc == HC - 1 and jc == DC - 1)
                                 else None)
                    for tt in range(2):
                        ti = half * 8 + tb2 * 2 + tt
                        use_guard("lnt_sca", "vector")
                        use_guard("oq1024", "vector")
                        wait("vector", "TEN")
                        for jc in range(DC):
                            po = P1024[jc // 4][:, (jc % 4) * 256 + tt * 128:
                                                (jc % 4) * 256 + (tt + 1) * 128]
                            emit("vector", lambda e, jc=jc, po=po: e.tensor_scalar(
                                w1_s[:, jc * 128:(jc + 1) * 128], po,
                                cells[:, 24 + jc:25 + jc], 0.0,
                                ALU.add, ALU.add), "VEC")
                        quant_cascade(512, w1_s[:, 0:512], "e5m10",
                                      oq1024[:, 0:512])
                        quant_cascade(512, w1_s[:, 512:1024], "e5m10",
                                      oq1024[:, 512:1024])
                        set_guard("w1_s", "VEC")
                        if tt == 1:
                            set_guard("po", "VEC")
                        use_guard("pt4", "tensor")
                        for half4 in range(2):
                            wait("tensor", "VEC")
                            for q4 in range(4):
                                jc = half4 * 4 + q4
                                emit("tensor", lambda e, jc=jc, q4=q4:
                                     e.transpose(
                                         P512[2][0:128, q4 * 128:(q4 + 1) * 128],
                                         oq1024[:, jc * 128:(jc + 1) * 128],
                                         eye_s[:, 0:128]),
                                     "TEN" if q4 == 3 else None)
                            wait("vector", "TEN")
                            for q4 in range(4):
                                jc = half4 * 4 + q4
                                emit("vector", lambda e, jc=jc, q4=q4:
                                     e.tensor_copy(
                                         lnt[:, jc * 128:(jc + 1) * 128],
                                         P512[2][0:128, q4 * 128:(q4 + 1) * 128]),
                                     "VEC")
                        set_guard("pt4", "VEC")
                        set_guard("oq1024", "TEN")
                        if half == 1 and tb2 == 3 and tt == 1:
                            dbg_tap("c15", {"lntC": lnt[:],
                                            "oqC": oq1024[:, 0:128],
                                            "hsC": w1_s[:, 0:512]})
                        ln_stats(lnt[:])
                        set_guard("lnt_sca", "SCA")
                        use_guard("res%d" % (tt % 2), "vector")
                        ln_apply(lnt[:], xt[tt % 2][:])
                        emit("vector", lambda e, ti=ti, tt=tt: e.tensor_add(
                            xt[tt % 2][:], xt[tt % 2][:],
                            x2[:, ti * D:(ti + 1) * D]), "VEC")
                        wait("gpsimd", "VEC")
                        emit("gpsimd", lambda e, ti=ti, tt=tt: e.dma_start(
                            out=out_h[ti * 128:(ti + 1) * 128, :],
                            in_=xt[tt % 2][:]), "GD", 16)
                        set_guard("res%d" % (tt % 2), "GD")

            dbg_tap("end", {"x2": x2[:], "lnt": lnt[:]})
            wait("gpsimd", "GD")

        @block.sync
        def _(E):
            program("sync", E)

        @block.tensor
        def _(E):
            program("tensor", E)

        @block.vector
        def _(E):
            program("vector", E)

        @block.scalar
        def _(E):
            program("scalar", E)

        @block.gpsimd
        def _(E):
            program("gpsimd", E)

    return nc


def _host_layout(L0_attn_w_qkv, L0_attn_w_o, L0_ffn_w1, L0_ffn_b1,
                 L0_ffn_w2, L0_ffn_b2):
    wqkv = np.asarray(L0_attn_w_qkv)
    wqk = np.ascontiguousarray(
        np.concatenate([wqkv[:, 0], wqkv[:, 1]], axis=2).reshape(NH * D, 2 * DH))
    wvp = np.ascontiguousarray(
        wqkv[:, 2].reshape(NH // 2, 2, D, DH).transpose(0, 2, 1, 3)
        .reshape(NH // 2 * D, 2 * DH))
    wo = np.ascontiguousarray(np.asarray(L0_attn_w_o, dtype=np.float32))
    w1t = np.ascontiguousarray(np.asarray(L0_ffn_w1, dtype=np.float32).T)
    b1r = np.ascontiguousarray(
        np.asarray(L0_ffn_b1, dtype=np.float32).reshape(HID // 128, 128).T)
    w2t = np.ascontiguousarray(np.asarray(L0_ffn_w2, dtype=np.float32).T)
    b2r = np.ascontiguousarray(
        np.asarray(L0_ffn_b2, dtype=np.float32).reshape(D // 128, 128).T)
    eye = np.zeros((128, 192), dtype=np.float32)
    eye[:, 0:128] = np.eye(128, dtype=np.float32)
    eye[64:128, 128:192] = np.eye(64, dtype=np.float32)
    eye[0, 128:192] = 1.0
    return dict(wqk=wqk.astype(np.float32), wvp=wvp.astype(np.float32),
                wo=wo, w1t=w1t, b1r=b1r, w2t=w2t, b2r=b2r, eye=eye)


def kernel(L0_attn_inputs, L0_attn_w_qkv, L0_attn_w_o, L0_ffn_w1, L0_ffn_b1,
           L0_ffn_w2, L0_ffn_b2):
    shared = _host_layout(L0_attn_w_qkv, L0_attn_w_o, L0_ffn_w1, L0_ffn_b1,
                          L0_ffn_w2, L0_ffn_b2)
    xs = np.asarray(L0_attn_inputs, dtype=np.float32)
    nc = build_nc()
    in_maps = [dict(shared, x=np.ascontiguousarray(xs[c])) for c in range(8)]
    res = run_bass_kernel_spmd(nc, in_maps, core_ids=list(range(8)))
    out = np.stack([np.asarray(res.results[c]["out"]) for c in range(8)], axis=0)
    return out.astype(np.float32)


# version 11 (final)
# speedup vs baseline: 1.1003x; 1.1003x over previous
import numpy as np
import concourse.bass as bass
import concourse.mybir as mybir
from concourse.bass_utils import run_bass_kernel_spmd
from contextlib import ExitStack

F32 = mybir.dt.float32
BF16 = mybir.dt.bfloat16
F8 = mybir.dt.float8e4
F16 = mybir.dt.float16
AF = mybir.ActivationFunctionType
ALU = mybir.AluOpType

T = 2048
D = 1024
NH = 16
DH = 64
HID = 4096
NTT = 16
DC = 8
HC = 32
TB = 4
TKC = 16
EPS = 1e-5
CSHIFT = 12.0

ENGS = ("sync", "tensor", "vector", "scalar", "gpsimd")


def build_nc(dbg=()):
    nc = bass.Bass("TRN2", target_bir_lowering=False, debug=False,
                   detect_race_conditions=False)

    x_h = nc.dram_tensor("x", [T, D], F32, kind="ExternalInput").ap()
    wqk_h = nc.dram_tensor("wqk", [NH * D, 2 * DH], F32, kind="ExternalInput").ap()
    wvp_h = nc.dram_tensor("wvp", [NH // 2 * D, 2 * DH], F32, kind="ExternalInput").ap()
    wo_h = nc.dram_tensor("wo", [D, D], F32, kind="ExternalInput").ap()
    w1t_h = nc.dram_tensor("w1t", [D, HID], F32, kind="ExternalInput").ap()
    b1r_h = nc.dram_tensor("b1r", [128, HID // 128], F32, kind="ExternalInput").ap()
    w2t_h = nc.dram_tensor("w2t", [HID, D], F32, kind="ExternalInput").ap()
    b2r_h = nc.dram_tensor("b2r", [128, D // 128], F32, kind="ExternalInput").ap()
    eye_h = nc.dram_tensor("eye", [128, 192], F32, kind="ExternalInput").ap()
    out_h = nc.dram_tensor("out", [T, D], F32, kind="ExternalOutput").ap()
    dbg_h = {}
    for name, tap, shape, dt_ in dbg:
        dbg_h[name] = nc.dram_tensor("dbg_" + name, shape, dt_,
                                     kind="ExternalOutput").ap()

    with ExitStack() as stk:
        BASE = (nc.sbuf_base + 31) // 32 * 32
        alloc_log = []

        def at(name, shape, dtype, offset):
            sz = int(np.prod(shape[1:])) * mybir.dt.size(dtype)
            assert BASE + offset + sz <= nc.sbuf_top, (name, offset, sz)
            alloc_log.append((name, offset, sz))
            return nc.alloc_sbuf_tensor_at(name, shape, dtype, offset=BASE + offset)

        eye_s = at("eye_s", [128, 192], F32, 0)
        cells = at("cells", [128, 128], F32, 768)
        b1_s = at("b1_s", [128, HID // 128], F32, 1280)
        pv_s = at("pv_s", [128, DH], F32, 1408)
        lnt = at("lnt", [128, D], F32, 1696)
        xt = [at("xt%d" % i, [128, D], F32, 5792 + i * 4096) for i in range(2)]
        vT_s = at("vT_s", [128, T], F32, 1696)
        C_A1, C_A2 = slice(12, 14), slice(14, 16)
        C_M, C_V = slice(16, 17), slice(17, 18)
        C_T1, C_SQ = slice(18, 19), slice(19, 20)
        C_R = slice(20, 21)
        C_REC = slice(21, 22)
        C_EPS, C_NSH = slice(22, 23), slice(23, 24)
        C_B2Q, C_B2 = slice(24, 32), slice(32, 40)

        R = 14016
        qA = at("qA", [128, T], BF16, R)
        qB = at("qB", [128, T], BF16, R + 4096)
        kp = at("kp", [128, T], BF16, R + 8192)
        wqk_s = at("wqk_s", [128, D], F32, R + 12288)
        wqk_hi = at("wqk_hi", [128, D], BF16, R + 16384)
        wqk_lo = at("wqk_lo", [128, D], BF16, R + 18432)
        wv_s = at("wv_s", [128, D], F32, R + 20480)
        wv_hi = at("wv_hi", [128, D], BF16, R + 24576)
        wv_lo = at("wv_lo", [128, D], BF16, R + 26624)
        v_tok = at("v_tok", [128, TKC * (DH + 1)], F32, R + 28672)
        stg = at("stg", [64, 512], BF16, R + 32832)
        vstg = at("vstg", [64, 128], F32, R + 33856)
        wo_s = at("wo_s", [128, D], F32, R)
        wo_hi = at("wo_hi", [128, D], BF16, R + 4096)
        wo_lo = at("wo_lo", [128, D], BF16, R + 6144)
        scr512 = at("scr512", [128, 512], F32, R + 8192)
        w1_s = at("w1_s", [128, D], F32, R)
        w1_hi = at("w1_hi", [128, D], BF16, R + 4096)
        w1_lo = at("w1_lo", [128, D], BF16, R + 6144)
        oq1024 = at("oq1024", [128, D], F32, R + 4096)
        w2_s = at("w2_s", [128, 512], F32, R + 8192)
        h_s = at("h_s", [128, 512], F32, R + 10240)
        oq_s = at("oq_s", [128, 128], F32, R + 12288)
        QO = R + 12800
        q_u = at("q_u", [128, 512], BF16, QO)
        q_s = at("q_s", [128, 512], BF16, QO + 1024)
        q_b = at("q_b", [128, 512], BF16, QO + 2048)
        q_t = at("q_t", [128, 512], BF16, QO + 3072)
        q_tS = at("q_tS", [128, 512], BF16, QO + 4096)
        q_inv = at("q_inv", [128, 512], BF16, QO + 5120)
        q_scl = at("q_scl", [128, 512], BF16, QO + 6144)
        q_sg = at("q_sg", [128, 512], BF16, QO + 7168)
        q_ax = at("q_ax", [128, 512], F32, QO + 8192)
        q_y0 = at("q_y0", [128, 512], F32, QO + 10240)
        q_yr = at("q_yr", [128, 512], F32, QO + 12288)
        q_rr = at("q_rr", [128, 512], F32, QO + 14336)
        q_f16 = at("q_f16", [128, 512], F16, QO + 16384)

        S1 = R + 34400
        ln1T_hi = at("ln1T_hi", [128, DC * T], BF16, S1)
        ln1T_lo = at("ln1T_lo", [128, DC * T], BF16, S1 + 32768)
        x2 = at("x2", [128, NTT * D], F32, S1)
        S2 = S1 + 65536
        attnT_hi = at("attnT_hi", [128, DC * T], BF16, S2)
        attnT_lo = at("attnT_lo", [128, DC * T], BF16, S2 + 32768)
        lnfT_hi = at("lnfT_hi", [128, DC * 1024], BF16, S2)
        lnfT_lo = at("lnfT_lo", [128, DC * 1024], BF16, S2 + 16384)
        hq_half = at("hq_half", [128, HC * 1024], F8, S2 + 32768)
        S3 = S2 + 65536
        e_s = at("e_s", [128, TKC * 512], F32, S3)
        w2q = at("w2q", [128, HC * D], F8, S3)

        P512 = [stk.enter_context(nc.psum_tensor("p512_%d" % i, [128, 512], F32))
                for i in range(4)]
        P1024 = [stk.enter_context(nc.psum_tensor("p1024_%d" % i, [128, 1024], F32))
                 for i in range(2)]

        sems = {k: stk.enter_context(nc.semaphore("sem_" + k))
                for k in ("LD", "GD", "TEN", "VEC", "SCA", "POO")}
        block = stk.enter_context(nc.Block())

        def program(me, E):
            cnt = {k: 0 for k in sems}
            guards = {}

            def emit(owner, fn, sem=None, k=1):
                ins = fn(E) if me == owner else None
                if sem is not None:
                    if ins is not None:
                        ins.then_inc(sems[sem], k)
                    cnt[sem] += k
                return ins

            def wait(owner, sem, val=None):
                if me == owner:
                    E.wait_ge(sems[sem], cnt[sem] if val is None else val)

            def set_guard(key, sem):
                guards[key] = (sem, cnt[sem])

            def use_guard(key, owner):
                if key in guards:
                    s, v = guards[key]
                    wait(owner, s, v)

            def dbg_tap(label, tensors):
                names = [n for n, tap, _s, _d in dbg if tap == label]
                if not names:
                    return
                wait("gpsimd", "VEC")
                wait("gpsimd", "SCA")
                wait("gpsimd", "TEN")
                for name, tap, shape, dt_ in dbg:
                    if tap != label:
                        continue
                    src = tensors[name]
                    emit("gpsimd", lambda e, name=name, src=src: e.dma_start(
                        out=dbg_h[name], in_=src), "GD", 16)
                for o in ENGS:
                    wait(o, "GD")

            def vpad(n=1):
                for _ in range(n):
                    emit("vector", lambda e: e.tensor_copy(
                        cells[:, 48:80], eye_s[:, 0:32]), "VEC")

            def split_fp32(src_ap, hi_ap, lo_ap):
                emit("vector", lambda e: e.tensor_copy(hi_ap, src_ap), "VEC")
                emit("vector", lambda e: e.scalar_tensor_tensor(
                    lo_ap, src_ap, 1.0, hi_ap, ALU.mult, ALU.subtract), "VEC")

            def quant_cascade(F, src_ap, kind, out_ap, srcf=None, outf=None):
                man = 3 if kind.startswith("e4m3") else 10
                ebase = -6 if kind.startswith("e4m3") else -14
                ebits = [8, 4, 2, 1] if man == 3 else [16, 8, 4, 2, 1]
                rbits = [8, 4, 2, 1] if man == 3 else \
                    [1024, 512, 256, 128, 64, 32, 16, 8, 4, 2, 1]
                signed = not kind.endswith("pos")

                def body(v, sl, src_ap, out_ap, stt_ok):
                    if signed:
                        v(lambda e: e.tensor_scalar(q_s[sl], src_ap,
                                                    -(2.0 ** 60), 0.0,
                                                    ALU.mult, ALU.add))
                        v(lambda e: e.tensor_scalar(q_b[sl], q_s[sl], 0.0, 1.0,
                                                    ALU.max, ALU.min))
                        v(lambda e: e.tensor_scalar(q_sg[sl], q_b[sl], -2.0,
                                                    1.0, ALU.mult, ALU.add))
                        v(lambda e: e.tensor_tensor(q_ax[sl], src_ap, q_sg[sl],
                                                    ALU.mult))
                        ax = q_ax[sl]
                    else:
                        ax = src_ap
                    v(lambda e: e.tensor_copy(q_u[sl], ax))
                    first = True
                    for c in ebits:
                        Tc = 2.0 ** (ebase + c)
                        Mc = 2.0 ** 40 / Tc
                        v(lambda e, Mc=Mc: e.tensor_scalar(
                            q_s[sl], q_u[sl], Mc, -(2.0 ** 40),
                            ALU.mult, ALU.add))
                        v(lambda e: e.tensor_scalar(q_b[sl], q_s[sl], 0.0, 1.0,
                                                    ALU.max, ALU.min))
                        v(lambda e, c=c: e.tensor_scalar(
                            q_t[sl], q_b[sl], float(2.0 ** (-c) - 1.0), 1.0,
                            ALU.mult, ALU.add))
                        v(lambda e, c=c: e.tensor_scalar(
                            q_tS[sl], q_b[sl], float(2.0 ** c - 1.0), 1.0,
                            ALU.mult, ALU.add))
                        v(lambda e: e.tensor_mul(q_u[sl], q_u[sl], q_t[sl]))
                        if first:
                            iv0 = float(2.0 ** (man - ebase))
                            v(lambda e, iv0=iv0: e.tensor_scalar(
                                q_inv[sl], q_t[sl], iv0, 0.0,
                                ALU.mult, ALU.add))
                            v(lambda e, iv0=iv0: e.tensor_scalar(
                                q_scl[sl], q_tS[sl], 1.0 / iv0, 0.0,
                                ALU.mult, ALU.add))
                            first = False
                        else:
                            v(lambda e: e.tensor_mul(q_inv[sl], q_inv[sl],
                                                     q_t[sl]))
                            v(lambda e: e.tensor_mul(q_scl[sl], q_scl[sl],
                                                     q_tS[sl]))
                    v(lambda e: e.tensor_mul(q_y0[sl], ax, q_inv[sl]))
                    srcy = q_y0
                    for c in rbits:
                        Mc = 2.0 ** 40 / c
                        v(lambda e, Mc=Mc, srcy=srcy: e.tensor_scalar(
                            q_s[sl], srcy[sl], Mc, -(2.0 ** 40),
                            ALU.mult, ALU.add))
                        v(lambda e: e.tensor_scalar(q_b[sl], q_s[sl], 0.0, 1.0,
                                                    ALU.max, ALU.min))
                        if c == 1:
                            v(lambda e: e.tensor_copy(q_tS[sl], q_b[sl]))
                        if stt_ok:
                            v(lambda e, c=c, srcy=srcy: e.scalar_tensor_tensor(
                                q_yr[sl], q_b[sl], -float(c), srcy[sl],
                                ALU.mult, ALU.add))
                        else:
                            v(lambda e, c=c: e.tensor_scalar(
                                q_t[sl], q_b[sl], -float(c), 0.0,
                                ALU.mult, ALU.add))
                            v(lambda e, srcy=srcy: e.tensor_tensor(
                                q_yr[sl], q_t[sl], srcy[sl], ALU.add))
                        srcy = q_yr
                    v(lambda e: e.tensor_scalar(q_s[sl], q_yr[sl], 2.0 ** 41,
                                                -(2.0 ** 40),
                                                ALU.mult, ALU.add))
                    v(lambda e: e.tensor_scalar(q_b[sl], q_s[sl], 0.0, 1.0,
                                                ALU.max, ALU.min))
                    v(lambda e: e.tensor_scalar(q_s[sl], q_yr[sl],
                                                -(2.0 ** 41), 2.0 ** 40,
                                                ALU.mult, ALU.add))
                    v(lambda e: e.tensor_scalar(q_t[sl], q_s[sl], 0.0, 1.0,
                                                ALU.max, ALU.min))
                    v(lambda e: e.tensor_scalar(q_t[sl], q_t[sl], -1.0, 1.0,
                                                ALU.mult, ALU.add))
                    v(lambda e: e.tensor_sub(q_t[sl], q_t[sl], q_b[sl]))
                    v(lambda e: e.tensor_mul(q_t[sl], q_t[sl], q_tS[sl]))
                    v(lambda e: e.tensor_add(q_b[sl], q_b[sl], q_t[sl]))
                    v(lambda e: e.tensor_sub(q_rr[sl], q_y0[sl], q_yr[sl]))
                    if stt_ok:
                        v(lambda e: e.scalar_tensor_tensor(
                            q_rr[sl], q_b[sl], 1.0, q_rr[sl],
                            ALU.mult, ALU.add))
                    else:
                        v(lambda e: e.tensor_add(q_rr[sl], q_rr[sl], q_b[sl]))
                    v(lambda e: e.tensor_mul(q_rr[sl], q_rr[sl], q_scl[sl]))
                    if signed:
                        v(lambda e: e.tensor_mul(q_rr[sl], q_rr[sl], q_sg[sl]))
                    if man == 3:
                        v(lambda e: e.tensor_copy(out_ap, q_rr[sl]))
                    else:
                        v(lambda e: e.tensor_copy(q_f16[sl], q_rr[sl]))
                        v(lambda e: e.tensor_copy(out_ap, q_f16[sl]))

                if True:
                    if src_ap is None:
                        src_ap = srcf(0, F)
                        out_ap = outf(0, F)

                    def v(f):
                        emit("vector", f, "VEC")
                        if F < 32:
                            vpad()
                    body(v, (slice(0, 128), slice(0, F)), src_ap, out_ap, True)
                    return
                SPL = F // 2
                for s_ in ("LD", "SCA", "VEC"):
                    wait("gpsimd", s_)
                body(lambda f: emit("vector", f, "VEC"),
                     (slice(0, 128), slice(0, SPL)),
                     srcf(0, SPL), outf(0, SPL), True)
                body(lambda f: emit("gpsimd", f, "POO"),
                     (slice(0, 128), slice(SPL, F)),
                     srcf(SPL, F), outf(SPL, F), False)
                wait("vector", "POO")
                vpad()

            def ln_stats(src_ap1024):
                wait("vector", "SCA")
                v = lambda f: emit("vector", f, "VEC")
                v(lambda e: e.bn_stats(cells[:, 0:6], src_ap1024[:, 0:512]))
                v(lambda e: e.bn_stats(cells[:, 6:12], src_ap1024[:, 512:1024]))
                v(lambda e: e.bn_aggr(cells[:, 12:14], cells[:, 0:6]))
                v(lambda e: e.bn_aggr(cells[:, 14:16], cells[:, 6:12]))
                v(lambda e: e.tensor_mul(cells[:, 18:19], cells[:, 12:13],
                                         cells[:, 12:13]))
                v(lambda e: e.tensor_mul(cells[:, 40:41], cells[:, 14:15],
                                         cells[:, 14:15]))
                v(lambda e: e.tensor_add(cells[:, 41:42], cells[:, 12:13],
                                         cells[:, 14:15]))
                v(lambda e: e.tensor_add(cells[:, 18:19], cells[:, 18:19],
                                         cells[:, 13:14]))
                v(lambda e: e.tensor_add(cells[:, 40:41], cells[:, 40:41],
                                         cells[:, 15:16]))
                v(lambda e: e.tensor_scalar(cells[:, C_M], cells[:, 41:42],
                                            0.5, 0.0, ALU.mult, ALU.add))
                v(lambda e: e.tensor_add(cells[:, 18:19], cells[:, 18:19],
                                         cells[:, 40:41]))
                v(lambda e: e.tensor_mul(cells[:, 19:20], cells[:, C_M],
                                         cells[:, C_M]))
                vpad()
                v(lambda e: e.scalar_tensor_tensor(
                    cells[:, C_V], cells[:, 18:19], 0.5, cells[:, 19:20],
                    ALU.mult, ALU.subtract))
                v(lambda e: e.tensor_copy(cells[:, 48:80], eye_s[:, 0:32]))
                wait("scalar", "VEC")
                emit("scalar", lambda e: e.activation(
                    cells[:, 42:43], cells[:, C_V], AF.Sqrt,
                    bias=cells[:, C_EPS], scale=1.0), "SCA")
                wait("vector", "SCA")
                vpad()
                v(lambda e: e.reciprocal(cells[:, C_R], cells[:, 42:43]))
                vpad()

            def ln_apply(src_ap1024, dst_ap1024):
                emit("vector", lambda e: e.tensor_scalar(
                    dst_ap1024, src_ap1024, cells[:, C_M], cells[:, C_R],
                    ALU.subtract, ALU.mult), "VEC")

            def transpose_split8(srcs, his, los):
                use_guard("pt", "tensor")
                for k in range(8):
                    emit("tensor", lambda e, k=k: e.transpose(
                        P1024[1][0:128, k * 128:(k + 1) * 128], srcs[k],
                        eye_s[:, 0:128]), "TEN" if k == 7 else None)
                wait("vector", "TEN")
                for k in range(8):
                    emit("vector", lambda e, k=k: e.tensor_copy(
                        his[k], P1024[1][0:128, k * 128:(k + 1) * 128]), "VEC")
                for k in range(8):
                    emit("vector", lambda e, k=k: e.scalar_tensor_tensor(
                        los[k], P1024[1][0:128, k * 128:(k + 1) * 128], 1.0,
                        his[k], ALU.mult, ALU.subtract), "VEC")
                set_guard("pt", "VEC")

            emit("sync", lambda e: e.dma_start(out=eye_s[:], in_=eye_h), "LD", 16)
            emit("sync", lambda e: e.dma_start(out=b1_s[:], in_=b1r_h), "LD", 16)
            emit("sync", lambda e: e.dma_start(out=cells[:, C_B2], in_=b2r_h),
                 "LD", 16)
            emit("vector", lambda e: e.memset(v_tok[:], 1.0), "VEC")
            emit("vector", lambda e: e.memset(cells[:, C_EPS], EPS), "VEC")
            emit("vector", lambda e: e.memset(cells[:, C_NSH], -CSHIFT), "VEC")

            for i in range(NTT):
                use_guard("xt%d" % (i % 2), "sync")
                emit("sync", lambda e, i=i: e.dma_start(
                    out=xt[i % 2][:], in_=x_h[i * 128:(i + 1) * 128, :]), "LD", 16)
                wait("vector", "LD")
                ln_stats(xt[i % 2][:])
                use_guard("lnt_rd", "vector")
                ln_apply(xt[i % 2][:], lnt[:])
                set_guard("xt%d" % (i % 2), "VEC")
                wait("tensor", "VEC")
                transpose_split8(
                    [lnt[:, k * 128:(k + 1) * 128] for k in range(DC)],
                    [ln1T_hi[:, k * T + i * 128:k * T + (i + 1) * 128]
                     for k in range(DC)],
                    [ln1T_lo[:, k * T + i * 128:k * T + (i + 1) * 128]
                     for k in range(DC)])
                set_guard("lnt_rd", "TEN")

            dbg_tap("p0", {"ln1T_hi": ln1T_hi[:], "ln1T_lo": ln1T_lo[:],
                           "lnt": lnt[:], "stb": cells[:, 0:24],
                           "st_r": cells[:, C_R], "xt1": xt[1][:]})
            for h in range(NH):
                par = h % 2
                use_guard("wqk_s", "sync")
                for k in range(DC):
                    emit("sync", lambda e, h=h, k=k: e.dma_start(
                        out=wqk_s[:, k * 128:(k + 1) * 128],
                        in_=wqk_h[h * D + k * 128:h * D + (k + 1) * 128, :]),
                        "LD", 16)
                use_guard("wqk_hi", "vector")
                wait("vector", "LD")
                split_fp32(wqk_s[:], wqk_hi[:], wqk_lo[:])
                set_guard("wqk_s", "VEC")
                if par == 0:
                    use_guard("wv_s", "sync")
                    for k in range(DC):
                        emit("sync", lambda e, h=h, k=k: e.dma_start(
                            out=wv_s[:, k * 128:(k + 1) * 128],
                            in_=wvp_h[(h // 2) * D + k * 128:
                                      (h // 2) * D + (k + 1) * 128, :]), "LD", 16)
                    use_guard("wv_hi", "vector")
                    wait("vector", "LD")
                    split_fp32(wv_s[:], wv_hi[:], wv_lo[:])
                    set_guard("wv_s", "VEC")
                wait("tensor", "VEC")
                for tb in range(TB):
                    tsl = slice(tb * 512, (tb + 1) * 512)
                    use_guard("pqk", "tensor")
                    nmm = 0
                    for wl, ll in ((wqk_hi, ln1T_hi), (wqk_hi, ln1T_lo),
                                   (wqk_lo, ln1T_hi)):
                        for k in range(DC):
                            nmm += 1
                            emit("tensor",
                                 lambda e, wl=wl, ll=ll, k=k, tb=tb, nmm=nmm:
                                 e.matmul(
                                     P512[0][:],
                                     wl[:, k * 128:(k + 1) * 128],
                                     ll[:, k * T + tb * 512:k * T + (tb + 1) * 512],
                                     start=(nmm == 1), stop=(nmm == 24)),
                                 "TEN" if nmm == 24 else None)
                    wait("vector", "TEN")
                    emit("vector", lambda e, tsl=tsl: e.tensor_copy(
                        qA[0:64, tsl], P512[0][0:64, :]), "VEC")
                    emit("vector", lambda e, tsl=tsl: e.tensor_copy(
                        qA[64:128, tsl], qA[0:64, tsl]), "VEC")
                    emit("vector", lambda e, tsl=tsl: e.scalar_tensor_tensor(
                        qB[0:64, tsl], P512[0][0:64, :], 1.0, qA[0:64, tsl],
                        ALU.mult, ALU.subtract), "VEC")
                    emit("vector", lambda e, tsl=tsl: e.tensor_copy(
                        qB[64:128, tsl], qB[0:64, tsl]), "VEC")
                    emit("vector", lambda e, tsl=tsl: e.tensor_copy(
                        kp[0:64, tsl], P512[0][64:128, :]), "VEC")
                    emit("vector", lambda e, tsl=tsl: e.scalar_tensor_tensor(
                        kp[64:128, tsl], P512[0][64:128, :], 1.0, kp[0:64, tsl],
                        ALU.mult, ALU.subtract), "VEC")
                    set_guard("pqk", "VEC")
                    if par == 0:
                        use_guard("pv", "tensor")
                        nmm = 0
                        for wl, ll in ((wv_hi, ln1T_hi), (wv_hi, ln1T_lo),
                                       (wv_lo, ln1T_hi)):
                            for k in range(DC):
                                nmm += 1
                                emit("tensor",
                                     lambda e, wl=wl, ll=ll, k=k, tb=tb, nmm=nmm:
                                     e.matmul(
                                         P1024[0][:, 0:512],
                                         wl[:, k * 128:(k + 1) * 128],
                                         ll[:, k * T + tb * 512:
                                            k * T + (tb + 1) * 512],
                                         start=(nmm == 1), stop=(nmm == 24)),
                                     "TEN" if nmm == 24 else None)
                        use_guard("vT_s", "vector")
                        wait("vector", "TEN")
                        emit("vector", lambda e, tsl=tsl: e.tensor_copy(
                            vT_s[:, tsl], P1024[0][:, 0:512]), "VEC")
                        set_guard("pv", "VEC")
                if par == 0:
                    set_guard("wv_hi", "TEN")
                set_guard("wqk_hi", "TEN")
                if h == 0:
                    dbg_tap("vearly", {"vT_sE": vT_s[:], "wv_hiE": wv_hi[:],
                                       "wv_loE": wv_lo[:], "wv_sE": wv_s[:]})
                use_guard("v_tok", "tensor")
                wait("tensor", "VEC")
                for c in range(TKC):
                    if par:
                        use_guard("vstg", "vector")
                        emit("vector", lambda e, c=c: e.tensor_copy(
                            vstg[0:64, 0:128],
                            vT_s[64:128, c * 128:(c + 1) * 128]), "VEC")
                        wait("tensor", "VEC")
                    use_guard("pt", "tensor")
                    emit("tensor", lambda e, c=c, par=par: e.transpose(
                        P1024[1][0:128, 0:64],
                        vstg[0:64, 0:128] if par else
                        vT_s[0:64, c * 128:(c + 1) * 128],
                        eye_s[0:64, 0:64]), "TEN")
                    if par:
                        set_guard("vstg", "TEN")
                    wait("vector", "TEN")
                    emit("vector", lambda e, c=c: e.tensor_copy(
                        v_tok[:, c * 65:c * 65 + 64], P1024[1][0:128, 0:64]),
                        "VEC")
                    set_guard("pt", "VEC")
                if par == 1:
                    set_guard("vT_s", "TEN")
                for tb in range(TB):
                    tsl = slice(tb * 512, (tb + 1) * 512)
                    wait("tensor", "VEC")
                    for c in range(TKC):
                        ps = P512[2 + (c % 2)]
                        use_guard("ps%d" % (c % 2), "tensor")
                        emit("tensor", lambda e, c=c, ps=ps, tsl=tsl: e.matmul(
                            ps[:], kp[:, c * 128:(c + 1) * 128], qA[:, tsl],
                            start=True, stop=False))
                        emit("tensor", lambda e, c=c, ps=ps, tsl=tsl: e.matmul(
                            ps[:], kp[:, c * 128:(c + 1) * 128], qB[:, tsl],
                            start=False, stop=True), "TEN")
                        wait("scalar", "TEN")
                        emit("scalar", lambda e, c=c, ps=ps: e.activation(
                            e_s[:, c * 512:(c + 1) * 512], ps[:], AF.Exp,
                            bias=cells[:, C_NSH], scale=0.125), "SCA")
                        set_guard("ps%d" % (c % 2), "SCA")
                    wait("tensor", "SCA")
                    use_guard("pv", "tensor")
                    for c in range(TKC):
                        emit("tensor", lambda e, c=c: e.matmul(
                            P1024[0][0:65, 0:512],
                            v_tok[:, c * 65:(c + 1) * 65],
                            e_s[:, c * 512:(c + 1) * 512],
                            start=(c == 0), stop=(c == TKC - 1)),
                            "TEN" if c == TKC - 1 else None)
                    use_guard("dnrow", "vector")
                    wait("vector", "TEN")
                    emit("vector", lambda e: e.tensor_copy(
                        xt[1][64:65, 0:512], P1024[0][64:65, 0:512]), "VEC")
                    vpad()
                    emit("vector", lambda e: e.reciprocal(
                        xt[1][0:1, 512:1024], xt[1][64:65, 0:512]), "VEC")
                    use_guard("ps0", "tensor")
                    wait("tensor", "VEC")
                    emit("tensor", lambda e: e.matmul(
                        P512[2][0:64, 0:512], eye_s[0:1, 128:192],
                        xt[1][0:1, 512:1024], start=True, stop=True), "TEN")
                    set_guard("dnrow", "TEN")
                    wait("vector", "TEN")
                    emit("vector", lambda e: e.tensor_copy(
                        xt[1][0:64, 0:512], P512[2][0:64, 0:512]), "VEC")
                    set_guard("ps0", "VEC")
                    coff0 = (h // 2) * T + tb * 512
                    use_guard("ps1", "vector")
                    if par == 0:
                        emit("vector", lambda e: e.tensor_tensor(
                            P512[3][0:64, 0:512], P1024[0][0:64, 0:512],
                            xt[1][0:64, 0:512], ALU.mult), "VEC")
                        emit("vector", lambda e, coff0=coff0: e.tensor_tensor(
                            attnT_hi[0:64, coff0:coff0 + 512],
                            P1024[0][0:64, 0:512], xt[1][0:64, 0:512],
                            ALU.mult), "VEC")
                        set_guard("pv", "VEC")
                        vpad()
                        emit("vector", lambda e, coff0=coff0:
                             e.scalar_tensor_tensor(
                                 attnT_lo[0:64, coff0:coff0 + 512],
                                 P512[3][0:64, 0:512], 1.0,
                                 attnT_hi[0:64, coff0:coff0 + 512],
                                 ALU.mult, ALU.subtract), "VEC")
                    else:
                        emit("vector", lambda e: e.tensor_tensor(
                            P512[3][0:64, 0:512], P1024[0][0:64, 0:512],
                            xt[1][0:64, 0:512], ALU.mult), "VEC")
                        emit("vector", lambda e: e.tensor_tensor(
                            stg[0:64, 0:512], P1024[0][0:64, 0:512],
                            xt[1][0:64, 0:512], ALU.mult), "VEC")
                        set_guard("pv", "VEC")
                        vpad()
                        emit("vector", lambda e, coff0=coff0: e.tensor_copy(
                            attnT_hi[64:128, coff0:coff0 + 512],
                            stg[0:64, 0:512]), "VEC")
                        emit("vector", lambda e: e.scalar_tensor_tensor(
                            stg[0:64, 0:512], P512[3][0:64, 0:512], 1.0,
                            stg[0:64, 0:512], ALU.mult, ALU.subtract), "VEC")
                        vpad()
                        emit("vector", lambda e, coff0=coff0: e.tensor_copy(
                            attnT_lo[64:128, coff0:coff0 + 512],
                            stg[0:64, 0:512]), "VEC")
                    set_guard("ps1", "VEC")

            for jc in range(DC):
                use_guard("wo_s", "sync")
                wait("sync", "TEN")
                for k in range(DC):
                    emit("sync", lambda e, jc=jc, k=k: e.dma_start(
                        out=wo_s[:, k * 128:(k + 1) * 128],
                        in_=wo_h[k * 128:(k + 1) * 128,
                                 jc * 128:(jc + 1) * 128]), "LD", 16)
                use_guard("wo_hi", "vector")
                wait("vector", "LD")
                split_fp32(wo_s[:], wo_hi[:], wo_lo[:])
                set_guard("wo_s", "VEC")
                wait("tensor", "VEC")
                for tb in range(TB):
                    use_guard("pwo", "tensor")
                    nmm = 0
                    for wl, al in ((wo_hi, attnT_hi), (wo_hi, attnT_lo),
                                   (wo_lo, attnT_hi)):
                        for k in range(DC):
                            nmm += 1
                            emit("tensor", lambda e, wl=wl, al=al, k=k, tb=tb,
                                 nmm=nmm: e.matmul(
                                     P512[0][:], wl[:, k * 128:(k + 1) * 128],
                                     al[:, k * T + tb * 512:k * T + (tb + 1) * 512],
                                     start=(nmm == 1), stop=(nmm == 24)),
                                 "TEN" if nmm == 24 else None)
                    use_guard("scr512", "vector")
                    wait("vector", "TEN")
                    emit("vector", lambda e: e.tensor_copy(scr512[:], P512[0][:]),
                         "VEC")
                    set_guard("pwo", "VEC")
                    if jc == 0 and tb == 0:
                        dbg_tap("wo0", {"scr512E": scr512[:]})
                    wait("tensor", "VEC")
                    for tt in range(4):
                        use_guard("pt", "tensor")
                        emit("tensor", lambda e, tt=tt: e.transpose(
                            P1024[1][0:128, 0:128],
                            scr512[:, tt * 128:(tt + 1) * 128],
                            eye_s[:, 0:128]), "TEN")
                        wait("vector", "TEN")
                        ti = tb * 4 + tt
                        emit("vector", lambda e, ti=ti, jc=jc: e.tensor_copy(
                            x2[:, ti * D + jc * 128:ti * D + (jc + 1) * 128],
                            P1024[1][0:128, 0:128]), "VEC")
                        set_guard("pt", "VEC")
                    set_guard("scr512", "TEN")
                set_guard("wo_hi", "TEN")
            dbg_tap("p2pre", {"x2pre": x2[:]})
            for i in range(NTT):
                use_guard("xt%d" % (i % 2), "gpsimd")
                wait("gpsimd", "TEN")
                emit("gpsimd", lambda e, i=i: e.dma_start(
                    out=xt[i % 2][:], in_=x_h[i * 128:(i + 1) * 128, :]), "GD", 16)
                dsl = slice(i * D, (i + 1) * D)
                wait("vector", "TEN")
                ln_stats(x2[:, dsl])
                use_guard("lnt_rd", "vector")
                ln_apply(x2[:, dsl], lnt[:])
                wait("vector", "GD")
                emit("vector", lambda e, dsl=dsl, i=i: e.tensor_add(
                    x2[:, dsl], lnt[:], xt[i % 2][:]), "VEC")
                set_guard("lnt_rd", "VEC")
                set_guard("xt%d" % (i % 2), "VEC")

            wait("sync", "TEN")
            wait("vector", "TEN")
            quant_cascade(D // 128, cells[:, C_B2], "e5m10", cells[:, C_B2Q])
            for half in range(2):
                hof = half * 1024
                for i in range(8):
                    ti = half * 8 + i
                    dsl = slice(ti * D, (ti + 1) * D)
                    ln_stats(x2[:, dsl])
                    use_guard("lnt_rd", "vector")
                    ln_apply(x2[:, dsl], lnt[:])
                    wait("tensor", "VEC")
                    transpose_split8(
                        [lnt[:, k * 128:(k + 1) * 128] for k in range(DC)],
                        [lnfT_hi[:, k * 1024 + i * 128:k * 1024 + (i + 1) * 128]
                         for k in range(DC)],
                        [lnfT_lo[:, k * 1024 + i * 128:k * 1024 + (i + 1) * 128]
                         for k in range(DC)])
                    set_guard("lnt_rd", "TEN")
                for hc in range(HC):
                    use_guard("w1_s", "sync")
                    for k in range(DC):
                        emit("sync", lambda e, hc=hc, k=k: e.dma_start(
                            out=w1_s[:, k * 128:(k + 1) * 128],
                            in_=w1t_h[k * 128:(k + 1) * 128,
                                      hc * 128:(hc + 1) * 128]), "LD", 16)
                    use_guard("w1_hi", "vector")
                    wait("vector", "LD")
                    split_fp32(w1_s[:], w1_hi[:], w1_lo[:])
                    set_guard("w1_s", "VEC")
                    wait("tensor", "VEC")
                    for tb in range(2):
                        pf = P512[tb % 2]
                        use_guard("ph%d" % (tb % 2), "tensor")
                        nmm = 0
                        for wl, ll in ((w1_hi, lnfT_hi), (w1_hi, lnfT_lo),
                                       (w1_lo, lnfT_hi)):
                            for k in range(DC):
                                nmm += 1
                                emit("tensor",
                                     lambda e, wl=wl, ll=ll, k=k, tb=tb, pf=pf,
                                     nmm=nmm: e.matmul(
                                         pf[:], wl[:, k * 128:(k + 1) * 128],
                                         ll[:, k * 1024 + tb * 512:
                                            k * 1024 + (tb + 1) * 512],
                                         start=(nmm == 1), stop=(nmm == 24)),
                                     "TEN" if nmm == 24 else None)
                        use_guard("h_s", "scalar")
                        wait("scalar", "TEN")
                        emit("scalar", lambda e, hc=hc, pf=pf: e.activation(
                            h_s[:], pf[:], AF.Relu, bias=b1_s[:, hc:hc + 1],
                            scale=1.0), "SCA")
                        set_guard("ph%d" % (tb % 2), "SCA")
                        wait("vector", "SCA")
                        quant_cascade(
                            512, None, "e4m3pos", None,
                            srcf=lambda a, b: h_s[:, a:b],
                            outf=lambda a, b, hc=hc, tb=tb: hq_half[
                                :, hc * 1024 + tb * 512 + a:
                                hc * 1024 + tb * 512 + b])
                        set_guard("h_s", "VEC")
                    set_guard("w1_hi", "TEN")
                    if half == 0:
                        use_guard("w2_s", "sync")
                        for piece in range(2):
                            emit("sync", lambda e, hc=hc, piece=piece: e.dma_start(
                                out=w2_s[:],
                                in_=w2t_h[hc * 128:(hc + 1) * 128,
                                          piece * 512:(piece + 1) * 512]),
                                "LD", 16)
                            wait("vector", "LD")
                            quant_cascade(
                                512, None, "e4m3", None,
                                srcf=lambda a, b: w2_s[:, a:b],
                                outf=lambda a, b, hc=hc, piece=piece: w2q[
                                    :, hc * D + piece * 512 + a:
                                    hc * D + piece * 512 + b])
                            set_guard("w2_s", "VEC")
                            use_guard("w2_s", "sync")
                if half == 0:
                    dbg_tap("p4a", {"hq_half": hq_full[:] if False else hq_half[:],
                                    "w2qE": w2q[:], "cellsE": cells[:]})
                for tb2 in range(4):
                    wait("tensor", "VEC")
                    use_guard("po", "tensor")
                    for hc in range(HC):
                        for jc in range(DC):
                            po = P1024[jc // 4][:, (jc % 4) * 256:
                                                (jc % 4) * 256 + 256]
                            emit("tensor", lambda e, hc=hc, jc=jc, tb2=tb2, po=po:
                                 e.matmul(
                                     po,
                                     w2q[:, hc * D + jc * 128:
                                         hc * D + (jc + 1) * 128],
                                     hq_half[:, hc * 1024 + tb2 * 256:
                                             hc * 1024 + (tb2 + 1) * 256],
                                     start=(hc == 0 and jc % 2 == 0),
                                     stop=(hc == HC - 1 and jc % 2 == 1),
                                     skip_group_check=True),
                                 "TEN" if (hc == HC - 1 and jc == DC - 1)
                                 else None)
                    for tt in range(2):
                        ti = half * 8 + tb2 * 2 + tt
                        use_guard("lnt_sca", "vector")
                        use_guard("oq1024", "vector")
                        wait("vector", "TEN")
                        for jc in range(DC):
                            po = P1024[jc // 4][:, (jc % 4) * 256 + tt * 128:
                                                (jc % 4) * 256 + (tt + 1) * 128]
                            emit("vector", lambda e, jc=jc, po=po: e.tensor_scalar(
                                w1_s[:, jc * 128:(jc + 1) * 128], po,
                                cells[:, 24 + jc:25 + jc], 0.0,
                                ALU.add, ALU.add), "VEC")
                        quant_cascade(512, None, "e5m10", None,
                                      srcf=lambda a, b: w1_s[:, a:b],
                                      outf=lambda a, b: oq1024[:, a:b])
                        quant_cascade(512, None, "e5m10", None,
                                      srcf=lambda a, b: w1_s[:, 512 + a:
                                                             512 + b],
                                      outf=lambda a, b: oq1024[:, 512 + a:
                                                               512 + b])
                        set_guard("w1_s", "VEC")
                        if tt == 1:
                            set_guard("po", "VEC")
                        use_guard("pt4", "tensor")
                        for half4 in range(2):
                            wait("tensor", "VEC")
                            for q4 in range(4):
                                jc = half4 * 4 + q4
                                emit("tensor", lambda e, jc=jc, q4=q4:
                                     e.transpose(
                                         P512[2][0:128, q4 * 128:(q4 + 1) * 128],
                                         oq1024[:, jc * 128:(jc + 1) * 128],
                                         eye_s[:, 0:128]),
                                     "TEN" if q4 == 3 else None)
                            wait("vector", "TEN")
                            for q4 in range(4):
                                jc = half4 * 4 + q4
                                emit("vector", lambda e, jc=jc, q4=q4:
                                     e.tensor_copy(
                                         lnt[:, jc * 128:(jc + 1) * 128],
                                         P512[2][0:128, q4 * 128:(q4 + 1) * 128]),
                                     "VEC")
                        set_guard("pt4", "VEC")
                        set_guard("oq1024", "TEN")
                        if half == 1 and tb2 == 3 and tt == 1:
                            dbg_tap("c15", {"lntC": lnt[:],
                                            "oqC": oq1024[:, 0:128],
                                            "hsC": w1_s[:, 0:512]})
                        ln_stats(lnt[:])
                        set_guard("lnt_sca", "SCA")
                        use_guard("res%d" % (tt % 2), "vector")
                        ln_apply(lnt[:], xt[tt % 2][:])
                        emit("vector", lambda e, ti=ti, tt=tt: e.tensor_add(
                            xt[tt % 2][:], xt[tt % 2][:],
                            x2[:, ti * D:(ti + 1) * D]), "VEC")
                        wait("gpsimd", "VEC")
                        emit("gpsimd", lambda e, ti=ti, tt=tt: e.dma_start(
                            out=out_h[ti * 128:(ti + 1) * 128, :],
                            in_=xt[tt % 2][:]), "GD", 16)
                        set_guard("res%d" % (tt % 2), "GD")

            dbg_tap("end", {"x2": x2[:], "lnt": lnt[:]})
            wait("gpsimd", "GD")

        @block.sync
        def _(E):
            program("sync", E)

        @block.tensor
        def _(E):
            program("tensor", E)

        @block.vector
        def _(E):
            program("vector", E)

        @block.scalar
        def _(E):
            program("scalar", E)

        @block.gpsimd
        def _(E):
            program("gpsimd", E)

    return nc


def _host_layout(L0_attn_w_qkv, L0_attn_w_o, L0_ffn_w1, L0_ffn_b1,
                 L0_ffn_w2, L0_ffn_b2):
    wqkv = np.asarray(L0_attn_w_qkv)
    wqk = np.ascontiguousarray(
        np.concatenate([wqkv[:, 0], wqkv[:, 1]], axis=2).reshape(NH * D, 2 * DH))
    wvp = np.ascontiguousarray(
        wqkv[:, 2].reshape(NH // 2, 2, D, DH).transpose(0, 2, 1, 3)
        .reshape(NH // 2 * D, 2 * DH))
    wo = np.ascontiguousarray(np.asarray(L0_attn_w_o, dtype=np.float32))
    w1t = np.ascontiguousarray(np.asarray(L0_ffn_w1, dtype=np.float32).T)
    b1r = np.ascontiguousarray(
        np.asarray(L0_ffn_b1, dtype=np.float32).reshape(HID // 128, 128).T)
    w2t = np.ascontiguousarray(np.asarray(L0_ffn_w2, dtype=np.float32).T)
    b2r = np.ascontiguousarray(
        np.asarray(L0_ffn_b2, dtype=np.float32).reshape(D // 128, 128).T)
    eye = np.zeros((128, 192), dtype=np.float32)
    eye[:, 0:128] = np.eye(128, dtype=np.float32)
    eye[64:128, 128:192] = np.eye(64, dtype=np.float32)
    eye[0, 128:192] = 1.0
    return dict(wqk=wqk.astype(np.float32), wvp=wvp.astype(np.float32),
                wo=wo, w1t=w1t, b1r=b1r, w2t=w2t, b2r=b2r, eye=eye)


def kernel(L0_attn_inputs, L0_attn_w_qkv, L0_attn_w_o, L0_ffn_w1, L0_ffn_b1,
           L0_ffn_w2, L0_ffn_b2):
    shared = _host_layout(L0_attn_w_qkv, L0_attn_w_o, L0_ffn_w1, L0_ffn_b1,
                          L0_ffn_w2, L0_ffn_b2)
    xs = np.asarray(L0_attn_inputs, dtype=np.float32)
    nc = build_nc()
    in_maps = [dict(shared, x=np.ascontiguousarray(xs[c])) for c in range(8)]
    res = run_bass_kernel_spmd(nc, in_maps, core_ids=list(range(8)))
    out = np.stack([np.asarray(res.results[c]["out"]) for c in range(8)], axis=0)
    return out.astype(np.float32)
